# version 27
# baseline (speedup 1.0000x reference)
import numpy as np
import ml_dtypes
import concourse.bass as bass
import concourse.mybir as mybir
from concourse.bass_utils import run_bass_kernel_spmd
from contextlib import ExitStack

F32 = mybir.dt.float32
BF16 = mybir.dt.bfloat16
AF = mybir.ActivationFunctionType
ALU = mybir.AluOpType
AX = mybir.AxisListType

B, T, D = 4, 4096, 1024
C = 128
NCH = T // C
FS = 512
MAGIC = 12582912.0
LN8 = 2.0794415416798357
EPS = 1e-5
PROJ = ("q", "k", "v", "be", "al")


class Tok:
    __slots__ = ("sem", "val", "dma")
    def __init__(self, dma=False):
        self.sem = None
        self.val = None
        self.dma = dma


class Sched:
    def __init__(self):
        self.recs = []

    def add(self, eng, fn, deps=(), dma=False, sem=None, dma_n=1,
            pe_fence=False):
        t = Tok(dma)
        flat = []
        for d in deps:
            if d is None:
                continue
            if isinstance(d, (list, tuple)):
                flat.extend(x for x in d if x is not None)
            else:
                flat.append(d)
        self.recs.append((eng, fn, flat, t, dma, sem if dma else eng, dma_n,
                          pe_fence))
        return t

    def resolve(self):
        counters = {}
        for eng, fn, deps, t, dma, sem, dma_n, pe_fence in self.recs:
            inc = 16 * dma_n if dma else 1
            counters[sem] = counters.get(sem, 0) + inc
            t.sem = sem
            t.val = counters[sem]

    def emit_one(self, name, engine, sems):
        seen = {}
        fence = None
        for eng, fn, deps, t, dma, sem, dma_n, pe_fence in self.recs:
            if eng != name:
                continue
            if fence is not None:
                engine.wait_ge(sems[name], fence.val)
                fence = None
            if pe_fence:
                fence = t
            need_drain = False
            for d in deps:
                if d.sem == name and not d.dma:
                    need_drain = True
                    continue
                if seen.get(d.sem, 0) >= d.val:
                    continue
                engine.wait_ge(sems[d.sem], d.val)
                seen[d.sem] = d.val
            if need_drain and name in ("vector", "scalar"):
                engine.drain()
            out = fn(engine)
            insts = out if isinstance(out, (list, tuple)) else [out]
            for inst in insts:
                inst.then_inc(sems[sem], 16 if dma else 1)


def _consts():
    c = {}
    c["ident_bf"] = np.eye(128, dtype=ml_dtypes.bfloat16)
    c["ident_f"] = np.eye(128, dtype=np.float32)
    ji = np.arange(128)
    mn = np.zeros((128, 128), np.float32)
    mn[ji[:, None] > ji[None, :]] = -1e9
    c["maskneg4_bf"] = np.tile(mn, (1, 4)).astype(ml_dtypes.bfloat16)
    c["ut64_f"] = np.triu(np.ones((128, 128), np.float32)) / 64.0
    c["msumw_f"] = np.full((128, 1), 1.0 / 64.0, np.float32)
    sel = np.zeros((24, 8 * 128), np.float32)
    for h in range(8):
        for k in (h, 8 + h, 16 + h):
            sel[k, 128 * h:128 * (h + 1)] = 1.0
    c["sel24_bf"] = sel.astype(ml_dtypes.bfloat16)
    c["ones1_f"] = np.ones((1, 128), np.float32)
    c["onesc_bf"] = np.ones((128, 1), dtype=ml_dtypes.bfloat16)
    c["cimm_f"] = np.tile(np.array([[EPS, -0.5, 0.5, -1.0]], np.float32),
                          (128, 1))
    return c


def build_program(nch=NCH, stop_after=None):
    import os as _os0
    nc = bass.Bass("TRN2", target_bir_lowering=False,
                   debug=bool(_os0.environ.get("DBGPTRB")))

    xt_d = nc.dram_tensor("xt", [T, D], F32, kind="ExternalInput").ap()
    xf_d = nc.dram_tensor("xf", [D, T], F32, kind="ExternalInput").ap()
    w_d = {p: nc.dram_tensor(f"w{p}", [D, FS], F32, kind="ExternalInput").ap()
           for p in PROJ}
    b_d = {p: nc.dram_tensor(f"b{p}", [1, FS], F32, kind="ExternalInput").ap()
           for p in PROJ}
    rhs_d = nc.dram_tensor("rhs", [D, FS], F32, kind="ExternalInput").ap()
    out_d = nc.dram_tensor("out", [T, FS], F32, kind="ExternalOutput").ap()

    cn = _consts()
    ct = {k: nc.inline_tensor(v, name=k) for k, v in cn.items()}

    sch = Sched()
    V, Sc, Tn, Sy, Gp = "vector", "scalar", "tensor", "sync", "gp"

    with ExitStack() as ctx:
        def sb(name, shape, dt=F32):
            return ctx.enter_context(nc.sbuf_tensor(name, shape, dt))

        IDENT = sb("IDENT", [128, 128], BF16)
        IDENTF = sb("IDENTF", [128, 128], F32)
        MASKN4 = sb("MASKN4", [128, 512], BF16)
        UT64 = sb("UT64", [128, 128], F32)
        MSUMW = sb("MSUMW", [128, 1], F32)
        SEL24 = sb("SEL24", [24, 1024], BF16)
        ONES1 = sb("ONES1", [1, 128], F32)
        ONESC = sb("ONESC", [128, 1], BF16)
        CIMM = sb("CIMM", [128, 4], F32)
        for i, v in enumerate([EPS, -0.5, 0.5, -1.0]):
            nc.const_aps.register(F32, v, CIMM[:, i:i + 1])

        WT = {p: sb(f"WT{p}", [128, 8 * FS], BF16) for p in PROJ}
        QRHS = sb("QRHS", [128, 8 * FS], BF16)
        FIXR = {p: sb(f"FIXR{p}", [4, FS], BF16) for p in PROJ}
        BSTG = sb("BSTG", [1, 3 * FS], F32)
        BHBF = sb("BHBF", [1, 2 * FS], BF16)
        RS4 = sb("RS4", [128, 8], F32)
        RSB = sb("RSB", [128, FS], F32)
        RROW = sb("RROW", [1, 2 * FS], F32)

        S = sb("S", [128, 4 * 64], F32)
        SBF = sb("SBF", [128, 4 * 64], BF16)

        def ps(name, shape):
            return ctx.enter_context(nc.psum_tensor(name, shape, F32))
        PB0 = ps("PB0", [128, 512])
        PB1 = ps("PB1", [128, 512])
        PB2 = ps("PB2", [128, 512])
        PB3 = ps("PB3", [128, 512])
        PB4 = ps("PB4", [128, 512])
        PB5 = ps("PB5", [128, 512])
        PB7 = ps("PB7", [128, 512])
        PTRB = ctx.enter_context(nc.psum_tensor("PTRB", [128, 1024], BF16))

        sems = {}
        for name in (Sy, Gp, V, Sc, Tn, "dconst", "dbias", "drow",
                     "dgp0", "dgp1", "din0", "din1", "din2", "din3",
                     "dout0", "dout1"):
            sems[name] = ctx.enter_context(nc.semaphore(f"sem_{name}"))
        block = ctx.enter_context(nc.Block())

        const_list = (("ident_bf", IDENT), ("ident_f", IDENTF),
                      ("maskneg4_bf", MASKN4), ("ut64_f", UT64),
                      ("msumw_f", MSUMW), ("sel24_bf", SEL24),
                      ("ones1_f", ONES1), ("onesc_bf", ONESC),
                      ("cimm_f", CIMM))
        def const_dmas(e):
            return [e.dma_start(t[:, :], ct[n].ap()) for n, t in const_list]
        tCONST = sch.add(Sy, const_dmas, dma=True, sem="dconst",
                         dma_n=len(const_list))

        tS0 = sch.add(V, lambda e: e.memset(S[:, :], 0.0))
        tSB0 = sch.add(V, lambda e: e.memset(SBF[:, :], 0.0))

        pb7_readers = []

        with ExitStack() as sctx:
            WSTG = [sctx.enter_context(nc.sbuf_tensor(f"WSTG{i}", [128, 8 * FS], F32))
                    for i in range(2)]
            QSC = sctx.enter_context(nc.sbuf_tensor("QSC", [128, 1024], F32))

            tWT = {}
            stg_free = [None, None]
            for i, p in enumerate(PROJ):
                stg = WSTG[i % 2]
                tD = sch.add(Gp, lambda e, p=p, stg=stg: e.dma_start(
                    stg[:, :].rearrange("p (c f) -> p c f", c=8),
                    w_d[p].rearrange("(c p) f -> p c f", p=128)),
                    deps=[stg_free[i % 2]], dma=True, sem=f"dgp{i % 2}")
                tC = sch.add(V, lambda e, p=p, stg=stg: e.tensor_copy(
                    WT[p][:, :], stg[:, :]), deps=[tD])
                tWT[p] = tC
                stg_free[i % 2] = tC
            tRD = sch.add(Gp, lambda e: e.dma_start(
                WSTG[1][:, :].rearrange("p (c f) -> p c f", c=8),
                rhs_d.rearrange("(c p) f -> p c f", p=128)),
                deps=[stg_free[1]], dma=True, sem="dgp1")

            tFIXR = {}
            tfix_parts = []
            prevb = None
            for p in PROJ:
                t1 = sch.add(Sy, lambda e, p=p: e.dma_start(
                    BSTG[0:1, 0:FS], b_d[p]), deps=[prevb], dma=True,
                    sem="dbias")
                t_h = sch.add(V, lambda e, p=p: e.tensor_copy(
                    BHBF[0:1, 0:FS], BSTG[0:1, 0:FS]), deps=[t1])
                t_b = sch.add(V, lambda e, p=p: e.tensor_copy(
                    BSTG[0:1, FS:2 * FS], BHBF[0:1, 0:FS]), deps=[t_h])
                t_l = sch.add(V, lambda e, p=p: e.tensor_sub(
                    BSTG[0:1, 2 * FS:3 * FS], BSTG[0:1, 0:FS],
                    BSTG[0:1, FS:2 * FS]), deps=[t_b])
                t_lb = sch.add(V, lambda e, p=p: e.tensor_copy(
                    BHBF[0:1, FS:2 * FS], BSTG[0:1, 2 * FS:3 * FS]),
                    deps=[t_l])
                tD1 = sch.add(Sy, lambda e, p=p: e.dma_start(
                    FIXR[p][1:2, :], BHBF[0:1, 0:FS]), deps=[t_h], dma=True,
                    sem="dbias")
                tD3 = sch.add(Sy, lambda e, p=p: e.dma_start(
                    FIXR[p][3:4, :], BHBF[0:1, 0:FS]), deps=[t_h], dma=True,
                    sem="dbias")
                tD2 = sch.add(Sy, lambda e, p=p: e.dma_start(
                    FIXR[p][2:3, :], BHBF[0:1, FS:2 * FS]),
                    deps=[t_lb], dma=True, sem="dbias")
                t_c2 = sch.add(V, lambda e: e.engine_nop(),
                               deps=[tD2, tD1, tD3])
                def sw_mms(e, p=p):
                    inst = None
                    for dc in range(8):
                        inst = e.matmul(PB7[0:1, 0:FS], ONESC[:, :],
                                        WT[p][:, 512 * dc:512 * (dc + 1)],
                                        start=(dc == 0), stop=(dc == 7))
                    return inst
                t_sw = sch.add(Tn, sw_mms, deps=[tWT[p], tCONST] + pb7_readers)
                t_swe = sch.add(V, lambda e, p=p: e.tensor_copy(
                    FIXR[p][0:1, :], PB7[0:1, 0:FS]), deps=[t_sw])
                pb7_readers = [t_swe]
                tFIXR[p] = [t_swe]
                tfix_parts.extend([tD2, t_swe])
                prevb = t_c2

            tFIXDONE = sch.add(V, lambda e: e.engine_nop(),
                               deps=tfix_parts)
            for p in PROJ:
                tFIXR[p] = [tFIXDONE]

            QTB = sctx.enter_context(nc.sbuf_tensor("QTB", [128, 4096], BF16))
            w0r = WSTG[0][:, :].rearrange("p (ft x) -> p ft x", ft=4)
            def rtr_f(e, dc):
                inst = None
                for ft in range(4):
                    inst = e.transpose(
                        PB7[:, 128 * ft:128 * (ft + 1)],
                        WSTG[1][:, 512 * dc + 128 * ft:512 * dc + 128 * (ft + 1)],
                        IDENTF[:, :])
                return inst
            tprev = [tRD, tWT["al"], tCONST] + pb7_readers
            for dc in range(8):
                tT = sch.add(Tn, lambda e, dc=dc: rtr_f(e, dc), deps=tprev, pe_fence=True)
                tE = sch.add(V, lambda e, dc=dc: e.tensor_copy(
                    w0r[:, :, 128 * dc:128 * (dc + 1)],
                    PB7[:, 0:512].rearrange("p (ft x) -> p ft x", ft=4)),
                    deps=[tT])
                tprev = [tE]
            tRT = tprev[0]
            pb7_readers = [tRT]
            def rs_f(e):
                for ft in range(4):
                    e.reduce_max(RS4[:, ft:ft + 1],
                                 WSTG[0][:, 1024 * ft:1024 * (ft + 1)],
                                 axis=AX.X, apply_absolute_value=True)
                e.drain()
                e.tensor_scalar(RS4[:, 0:4], RS4[:, 0:4], 1e-12, 1.0 / 127.0,
                                ALU.max, ALU.mult)
                e.drain()
                return e.reciprocal(RS4[:, 4:8], RS4[:, 0:4])
            tRS = sch.add(V, rs_f, deps=[tRT])
            def q_f(e):
                inst = None
                for ft in range(4):
                    e.drain()
                    e.tensor_scalar(WSTG[1][:, 0:1024],
                                    WSTG[0][:, 1024 * ft:1024 * (ft + 1)],
                                    RS4[:, 4 + ft:5 + ft], None, ALU.mult)
                    e.drain()
                    inst = e.tensor_scalar(QTB[:, 1024 * ft:1024 * (ft + 1)],
                                           WSTG[1][:, 0:1024], MAGIC, -MAGIC,
                                           ALU.add, ALU.add)
                return inst
            tQF = sch.add(V, q_f, deps=[tRS])
            tprev = [tQF, tRT]
            for dc in range(8):
                def btr_f(e, dc=dc):
                    inst = None
                    for ft in range(4):
                        inst = e.transpose(
                            PTRB[:, 128 * ft:128 * (ft + 1)],
                            QTB[:, 1024 * ft + 128 * dc:1024 * ft + 128 * (dc + 1)],
                            IDENT[:, :])
                    return inst
                tT = sch.add(Tn, btr_f, deps=tprev, pe_fence=True)
                tE = sch.add(V, lambda e, dc=dc: e.tensor_copy(
                    QRHS[:, 512 * dc:512 * (dc + 1)],
                    PTRB[:, 0:512]), deps=[tT])
                tprev = [tE]
            tQBACK = tprev[0]
            pb7_readers = [tQBACK]
            RSRT = sctx.enter_context(nc.sbuf_tensor("RSRT", [4, 128], F32))
            tTR = sch.add(Tn, lambda e: e.transpose(
                PB7[0:4, 0:128], RS4[:, 0:4], IDENTF[:, :]),
                deps=[tRS, tQBACK] + pb7_readers, pe_fence=True)
            tTRE = sch.add(V, lambda e: e.tensor_copy(
                RSRT[:, :], PB7[0:4, 0:128]), deps=[tTR])
            pb7_readers = [tTRE]
            tROWD = sch.add(Sy, lambda e: e.dma_start(
                RROW[0:1, 0:512].rearrange("o (a b) -> o a b", a=4),
                RSRT[:, :]), deps=[tTRE], dma=True, sem="drow")
            tBC1 = sch.add(Tn, lambda e: e.matmul(
                PB7[:, 0:FS], ONES1[:, :], RROW[0:1, 0:FS],
                start=True, stop=True), deps=[tROWD, tCONST] + pb7_readers)
            tRSB = sch.add(V, lambda e: e.tensor_copy(RSB[:, :], PB7[:, 0:FS]),
                           deps=[tBC1])
            pb7_readers = [tRSB]
            tQR = sch.add(V, lambda e: e.engine_nop(), deps=[tQBACK, tRSB])

        def db(name, shape, dt=F32, n=2):
            return [sb(f"{name}{i}", shape, dt) for i in range(n)]
        XT = db("XT", [128, D], n=4)
        XF = db("XF", [128, D], n=4)
        XFB = db("XFB", [128, D], BF16, n=3)
        STAT = db("STAT", [128, 16], n=3)
        BN6 = db("BN6", [128, 12], n=3)
        SCR8 = db("SCR8", [128, 8])
        PACKA = db("PACKA", [128, 4], BF16, n=3)
        FIXL = db("FIXL", [4, 128], BF16, n=3)
        QT2 = db("QT2", [128, D])
        QLR = db("QLR", [128, D], BF16, n=3)
        QLF = db("QLF", [128, D], BF16)
        VS = db("VS", [128, FS])
        BETA = db("BETA", [128, FS])
        EB = db("EB", [128, FS])
        ZA = db("ZA", [128, FS])
        ZC = db("ZC", [128, FS])
        EA = db("EA", [128, FS])
        PML = db("PML", [128, FS])
        PM8 = db("PM8", [128, 8])
        BM8 = db("BM8", [128, 8])
        MSB = db("MSB", [128, 8])
        CURVE = db("CURVE", [128, 8])
        CR = db("CR", [128, 8])
        BMR = db("BMR", [128, 8])
        RV = db("RV", [128, 8])
        CV = db("CV", [128, 8])
        PACKB = db("PACKB", [128, 24], BF16)
        RVT = db("RVT", [24, 128], BF16)
        AROW = db("AROW", [1, 8])
        ASB = db("ASB", [128, 4])
        QFM = db("QFM", [128, FS], BF16)
        KFM = db("KFM", [128, FS], BF16)
        KTM = db("KTM", [128, FS], BF16)
        VB = db("VB", [128, FS], BF16)
        VBT = db("VBT", [128, FS])
        ECV = db("ECV", [128, 8])
        VB2 = db("VB2", [128, FS], BF16)
        ESB = db("ESB", [128, 8 * 128])
        ATS = db("ATS", [128, 8 * 128], BF16)
        O1 = db("O1", [128, FS])
        Y2 = db("Y2", [128, FS])
        E2 = db("E2", [128, FS])
        T3 = db("T3", [128, FS])
        OUTB = db("OUTB", [128, FS])

        xt_ch = xt_d.rearrange("(n p) d -> n p d", p=128)
        out_ch = out_d.rearrange("(n p) f -> n p f", p=128)

        last_use = {}

        def reads(buf_slot, *toks):
            last_use.setdefault(id(buf_slot), []).extend(
                t for t in toks if t is not None)

        def war(buf_list, par):
            return last_use.pop(id(buf_list[par]), [])

        pw = {"PB0": [], "PB1": [], "PB2": [], "PB3": [], "PB4": [],
              "PB5": [], "PB7": pb7_readers, "PTRB": []}

        tS_prev = tS0
        tSBF_prev = tSB0
        tOUT_dma = None
        tOUT_dma2 = None

        tINs = {}

        def dma(cidx):
            p4 = cidx % 4
            startup_gate = [tQR, tROWD] if cidx < 4 else []

            def in_dmas(e, c=cidx, p=p4):
                return [
                    e.dma_start(XT[p][:, :], xt_ch[c]),
                    e.dma_start(
                        XF[p][:, :].rearrange("p (c t) -> p c t", c=8),
                        xf_d.rearrange("(c p) t -> p c t", p=128)
                        [:, :, 128 * c:128 * (c + 1)]),
                ]
            tINs[cidx] = sch.add(
                Sy, in_dmas, deps=war(XT, p4) + war(XF, p4) + startup_gate,
                dma=True, sem=f"din{p4}", dma_n=2)

        def stats(cidx):
            par = cidx % 3
            p4 = cidx % 4
            tXT = tXF = tINs[cidx]

            def bn_f(e, p=par, x=p4):
                e.bn_stats(BN6[p][:, 0:6], XT[x][:, 0:512])
                return e.bn_stats(BN6[p][:, 6:12], XT[x][:, 512:1024])
            tBN = sch.add(V, bn_f, deps=[tXT] + war(BN6, par))
            tAG = sch.add(V, lambda e, p=par: e.bn_aggr(
                STAT[p][:, 0:2], BN6[p][:, :]), deps=[tBN] + war(STAT, par))
            tMX = sch.add(V, lambda e, p=par, x=p4: e.reduce_max(
                STAT[p][:, 6:7], XT[x][:, :], axis=AX.X), deps=[tXT, tAG])
            tMN = sch.add(V, lambda e, p=par, x=p4: e.tensor_reduce(
                STAT[p][:, 7:8], XT[x][:, :], axis=AX.X, op=ALU.min),
                deps=[tXT, tAG])
            reads(XT[p4], tBN, tMX, tMN)

            tLN_ = sch.add(Sc, lambda e, p=par: e.activation(
                STAT[p][:, 2:3], STAT[p][:, 1:2], AF.Ln, bias=EPS), deps=[tAG])
            tR_ = sch.add(Sc, lambda e, p=par: e.activation(
                STAT[p][:, 3:4], STAT[p][:, 2:3], AF.Exp, scale=-0.5), deps=[tLN_])
            tSQ_ = sch.add(Sc, lambda e, p=par: e.activation(
                STAT[p][:, 4:5], STAT[p][:, 2:3], AF.Exp, scale=0.5), deps=[tLN_])

            def tiny1(e, p=par):
                st = STAT[p]
                e.tensor_scalar(st[:, 5:6], st[:, 2:3], -0.5, -LN8,
                                ALU.mult, ALU.add)
                e.tensor_sub(st[:, 8:9], st[:, 6:7], st[:, 0:1])
                e.tensor_sub(st[:, 9:10], st[:, 0:1], st[:, 7:8])
                e.drain()
                e.tensor_max(st[:, 8:9], st[:, 8:9], st[:, 9:10])
                e.drain()
                e.tensor_mul(st[:, 8:9], st[:, 8:9], st[:, 3:4])
                e.drain()
                e.tensor_scalar(st[:, 9:10], st[:, 8:9], 1e-12, 1.0 / 127.0,
                                ALU.max, ALU.mult)
                e.drain()
                return e.reciprocal(st[:, 10:11], st[:, 9:10])
            tTY1 = sch.add(V, tiny1, deps=[tMX, tMN, tR_])
            def tiny2(e, p=par):
                st = STAT[p]
                e.tensor_mul(st[:, 11:12], st[:, 3:4], st[:, 10:11])
                e.drain()
                e.tensor_mul(st[:, 12:13], st[:, 0:1], st[:, 11:12])
                e.drain()
                return e.tensor_scalar(st[:, 12:13], st[:, 12:13], -1.0, None,
                                       ALU.mult)
            tTY2 = sch.add(V, tiny2, deps=[tTY1])

            def packa_f(e, p=par):
                st = STAT[p]; pa = PACKA[p]
                e.tensor_scalar(pa[:, 0:1], st[:, 0:1], -1.0, None, ALU.mult)
                e.tensor_copy(pa[:, 1:2], st[:, 4:5])
                e.drain()
                e.tensor_copy(pa[:, 2:3], pa[:, 1:2])
                return e.tensor_sub(pa[:, 3:4], st[:, 4:5], pa[:, 1:2])
            tPA = sch.add(V, packa_f,
                          deps=[tSQ_, tTY2] + war(PACKA, par))
            reads(STAT[par], tLN_, tR_, tSQ_, tTY1, tTY2, tPA)

            tXFB = sch.add(Sc, lambda e, p=par, x=p4: e.activation(
                XFB[p][:, :], XF[x][:, :], AF.Copy),
                deps=[tXF] + war(XFB, par))
            reads(XF[p4], tXFB)

            p2 = cidx % 2
            tQT2 = sch.add(Sc, lambda e, p=p2, s=par, x=p4: e.activation(
                QT2[p][:, :], XT[x][:, :], AF.Identity, bias=STAT[s][:, 12:13],
                scale=STAT[s][:, 11:12]), deps=[tXT, tTY2] + war(QT2, p2))
            reads(XT[p4], tQT2)
            reads(STAT[par], tQT2)
            tQLR = sch.add(V, lambda e, p=par, q=p2: e.tensor_scalar(
                QLR[p][:, :], QT2[q][:, :], MAGIC, -MAGIC, ALU.add, ALU.add),
                deps=[tQT2] + war(QLR, par))
            reads(QT2[p2], tQLR)

            return dict(tXT=tXT, tR_=tR_, tTY1=tTY1, tTY2=tTY2, tPA=tPA,
                        tXFB=tXFB, tQLR=tQLR)

        def fixtrail(cidx, F):
            par = cidx % 3
            tTA = sch.add(Tn, lambda e, p=par: e.transpose(
                PTRB[0:4, 0:128], PACKA[p][:, :], IDENT[:, :]),
                deps=[F["tPA"], tCONST] + pw["PTRB"], pe_fence=True)
            pw["PTRB"] = []
            tFXL = sch.add(V, lambda e, p=par: e.tensor_copy(
                FIXL[p][:, :], PTRB[0:4, 0:128]), deps=[tTA] + war(FIXL, par))
            pw["PTRB"].append(tFXL)
            reads(PACKA[par], tTA)
            F["tFXL"] = tFXL

        def main(cidx, F):
            nonlocal tS_prev, tSBF_prev, tOUT_dma, tOUT_dma2
            par = cidx % 2
            p3 = cidx % 3
            p4 = cidx % 4
            tXT = F["tXT"]; tR_ = F["tR_"]; tTY1 = F["tTY1"]
            tTY2 = F["tTY2"]; tFXL = F["tFXL"]; tXFB = F["tXFB"]
            tm_toks = {}
            for pname, bank in (("al", PB2), ("v", PB0), ("be", PB1)):
                def tm_mms(e, p=p3, pn=pname, bank=bank):
                    for dc in range(8):
                        e.matmul(bank[:, :], XFB[p][:, 128 * dc:128 * (dc + 1)],
                                 WT[pn][:, 512 * dc:512 * (dc + 1)],
                                 start=(dc == 0), stop=False)
                    return e.matmul(bank[:, :], FIXL[p][:, :], FIXR[pn][:, :],
                                    start=False, stop=True)
                nm = "PB" + str(("v", "be", "al").index(pname))
                tm_toks[pname] = sch.add(
                    Tn, tm_mms,
                    deps=[tXFB, tFXL, tWT[pname]] + tFIXR[pname] + pw[nm])
                pw[nm] = []

            fm_toks = {}
            for pname, bank, nm in (("q", PB3, "PB3"), ("k", PB4, "PB4")):
                def fm_mms(e, p=p3, pn=pname, bank=bank):
                    inst = None
                    for fc in range(4):
                        for dc in range(8):
                            e.matmul(bank[:, 128 * fc:128 * (fc + 1)],
                                     WT[pn][:, 512 * dc + 128 * fc:
                                            512 * dc + 128 * (fc + 1)],
                                     XFB[p][:, 128 * dc:128 * (dc + 1)],
                                     start=(dc == 0), stop=False)
                        inst = e.matmul(bank[:, 128 * fc:128 * (fc + 1)],
                                        FIXR[pn][:, 128 * fc:128 * (fc + 1)],
                                        FIXL[p][:, :], start=False, stop=True)
                    return inst
                fm_toks[pname] = sch.add(
                    Tn, fm_mms,
                    deps=[tXFB, tFXL, tWT[pname]] + tFIXR[pname] + pw[nm])
                pw[nm] = []

            tZA = sch.add(Sc, lambda e, p=par, s=p3: e.activation(
                ZA[p][:, :], PB2[:, :], AF.Copy, scale=STAT[s][:, 3:4]),
                deps=[tm_toks["al"], tR_] + war(ZA, par))
            pw["PB2"].append(tZA)
            tVS = sch.add(Sc, lambda e, p=par, s=p3: e.activation(
                VS[p][:, :], PB0[:, :], AF.Copy, scale=STAT[s][:, 3:4]),
                deps=[tm_toks["v"], tR_] + war(VS, par))
            pw["PB0"].append(tVS)
            tZC = sch.add(V, lambda e, p=par: e.tensor_scalar(
                ZC[p][:, :], ZA[p][:, :], 10.0, -10.0, ALU.min, ALU.max),
                deps=[tZA] + war(ZC, par))
            reads(ZA[par], tZC)
            tEA = sch.add(Sc, lambda e, p=par: e.activation(
                EA[p][:, :], ZC[p][:, :], AF.Exp, scale=-1.0),
                deps=[tZC] + war(EA, par))
            reads(ZC[par], tEA)
            tEB = sch.add(Sc, lambda e, p=par, s=p3: e.activation(
                EB[p][:, :], PB1[:, :], AF.Exp, scale=STAT[s][:, 3:4]),
                deps=[tm_toks["be"], tR_] + war(EB, par))
            pw["PB1"].append(tEB)
            tPML = sch.add(Sc, lambda e, p=par: e.activation(
                PML[p][:, :], EA[p][:, :], AF.Ln, bias=1.0),
                deps=[tEA] + war(PML, par))
            reads(EA[par], tPML)

            tBE = sch.add(Sc, lambda e, p=par: e.activation(
                BETA[p][:, :], EB[p][:, :], AF.Ln, bias=1.0),
                deps=[tEB] + war(BETA, par))
            reads(EB[par], tBE)
            tPM8 = sch.add(V, lambda e, p=par: e.reduce_sum(
                PM8[p][:, :], PML[p][:, :].rearrange("p (h d) -> p h d", h=8),
                axis=AX.X), deps=[tPML] + war(PM8, par))
            reads(PML[par], tPM8)
            tBM8 = sch.add(V, lambda e, p=par: e.reduce_sum(
                BM8[p][:, :], BETA[p][:, :].rearrange("p (h d) -> p h d", h=8),
                axis=AX.X), deps=[tBE] + war(BM8, par))
            tQFM = sch.add(Sc, lambda e, p=par: e.activation(
                QFM[p][:, :], PB3[:, :], AF.Copy),
                deps=[fm_toks["q"]] + war(QFM, par))
            pw["PB3"].append(tQFM)
            tKFM = sch.add(Sc, lambda e, p=par: e.activation(
                KFM[p][:, :], PB4[:, :], AF.Copy),
                deps=[fm_toks["k"]] + war(KFM, par))
            pw["PB4"].append(tKFM)

            tQLR = F["tQLR"]
            def ql_tr(e, p=p3):
                e.nop()
                inst = None
                for dc in range(8):
                    inst = e.transpose(
                        PTRB[:, 128 * dc:128 * (dc + 1)],
                        QLR[p][:, 128 * dc:128 * (dc + 1)], IDENT[:, :])
                return inst
            tQTR = sch.add(Tn, ql_tr, deps=[tQLR] + pw["PTRB"], pe_fence=True)
            pw["PTRB"] = []
            tQLF = sch.add(Sc, lambda e, p=par: e.activation(
                QLF[p][:, :], PTRB[:, :], AF.Copy),
                deps=[tQTR] + war(QLF, par))
            pw["PTRB"].append(tQLF)
            reads(QLR[p3], tQTR)

            def ktm_mms(e, p=p3):
                for dc in range(8):
                    e.matmul(PB0[:, :], XFB[p][:, 128 * dc:128 * (dc + 1)],
                             WT["k"][:, 512 * dc:512 * (dc + 1)],
                             start=(dc == 0), stop=False)
                return e.matmul(PB0[:, :], FIXL[p][:, :], FIXR["k"][:, :],
                                start=False, stop=True)
            tKMM = sch.add(Tn, ktm_mms,
                           deps=[tXFB, tFXL, tWT["k"]] + tFIXR["k"] + pw["PB0"])
            pw["PB0"] = []
            tKTM = sch.add(Sc, lambda e, p=par: e.activation(
                KTM[p][:, :], PB0[:, :], AF.Copy),
                deps=[tKMM] + war(KTM, par))
            pw["PB0"].append(tKTM)

            def cums(e, p=par):
                e.matmul(PB7[:, 128:136], UT64[:, :], PM8[p][:, :],
                         start=True, stop=True)
                return e.matmul(PB7[0:1, 136:144], MSUMW[:, :], PM8[p][:, :],
                                start=True, stop=True)
            tM = sch.add(Tn, cums, deps=[tPM8] + pw["PB7"])
            pw["PB7"] = []
            reads(PM8[par], tM)
            tMSB = sch.add(V, lambda e, p=par: e.tensor_copy(
                MSB[p][:, :], PB7[:, 128:136]), deps=[tM] + war(MSB, par))
            tAR = sch.add(Sc, lambda e, p=par: e.activation(
                AROW[p][:, :], PB7[0:1, 136:144], AF.Exp, scale=-1.0),
                deps=[tM] + war(AROW, par))
            pw["PB7"] += [tMSB, tAR]
            def a_mms(e, p=par):
                e.matmul(PB7[0:64, 272:276], ONES1[0:1, 0:64],
                         AROW[p][0:1, 0:8:2], start=True, stop=True)
                e.nop()
                return e.matmul(PB7[64:128, 272:276], ONES1[0:1, 0:64],
                                AROW[p][0:1, 1:8:2], start=True, stop=True,
                                tile_position=(0, 64))
            tA = sch.add(Tn, a_mms, deps=[tAR])
            reads(AROW[par], tA)
            tAE = sch.add(V, lambda e, p=par: e.tensor_copy(
                ASB[p][:, :], PB7[:, 272:276]), deps=[tA] + war(ASB, par))
            pw["PB7"].append(tAE)
            tCUR = sch.add(Sc, lambda e, p=par: e.activation(
                CURVE[p][:, :], MSB[p][:, :], AF.Exp, scale=-1.0),
                deps=[tMSB] + war(CURVE, par))
            def tiny3(e, p=par, s=p3):
                e.tensor_scalar(CR[p][:, :], CURVE[p][:, :], STAT[s][:, 3:4],
                                0.125, ALU.mult, ALU.mult)
                e.tensor_scalar(BMR[p][:, :], BM8[p][:, :], STAT[s][:, 3:4],
                                0.125 / 64.0, ALU.mult, ALU.mult)
                e.tensor_scalar(RV[p][:, :], MSB[p][:, :], -1.0, STAT[s][:, 5:6],
                                ALU.mult, ALU.add)
                return e.tensor_scalar(CV[p][:, :], MSB[p][:, :], 1.0,
                                       STAT[s][:, 5:6], ALU.mult, ALU.add)
            tTY3 = sch.add(V, tiny3,
                           deps=[tCUR, tMSB, tBM8, tTY1] + war(CR, par)
                           + war(BMR, par) + war(RV, par) + war(CV, par))
            reads(CURVE[par], tTY3)
            reads(BM8[par], tTY3)
            def packb_f(e, p=par):
                pb = PACKB[p]; rv = RV[p]; sc8 = SCR8[p]
                e.tensor_copy(pb[:, 0:8], rv[:, :])
                e.drain()
                e.tensor_sub(sc8[:, 0:8], rv[:, :], pb[:, 0:8])
                e.drain()
                e.tensor_copy(pb[:, 8:16], sc8[:, 0:8])
                e.drain()
                return e.tensor_sub(pb[:, 16:24], sc8[:, 0:8], pb[:, 8:16])
            tPB_ = sch.add(V, packb_f, deps=[tTY3] + war(PACKB, par))
            reads(RV[par], tPB_)
            reads(SCR8[par], tPB_)
            def ffn_mms(e, p=par):
                inst = None
                for dc in range(8):
                    inst = e.matmul(PB4[:, :], QLF[p][:, 128 * dc:128 * (dc + 1)],
                                    QRHS[:, 512 * dc:512 * (dc + 1)],
                                    start=(dc == 0), stop=(dc == 7))
                return inst
            tFFN = sch.add(Tn, ffn_mms, deps=[tQLF, tQR] + pw["PB4"])
            pw["PB4"] = []
            reads(QLF[par], tFFN)
            tE2 = sch.add(V, lambda e, p=par, s=p3: e.scalar_tensor_tensor(
                E2[p][:, :], PB4[:, :], STAT[s][:, 9:10], RSB[:, :],
                ALU.mult, ALU.mult), deps=[tFFN, tTY1, tRSB] + war(E2, par))
            pw["PB4"].append(tE2)

            def attn_grp(e, half, bank, par_sel, p=par):
                inst = None
                for hh_ in range(par_sel, 4, 2):
                    h = 4 * half + hh_
                    bp = 64 * (h % 2)
                    inst = e.matmul(
                        bank[:, 128 * hh_:128 * (hh_ + 1)],
                        KFM[p][bp:bp + 64,
                               128 * (h // 2):128 * (h // 2) + 128],
                        QFM[p][bp:bp + 64,
                               128 * (h // 2):128 * (h // 2) + 128],
                        start=True, stop=True)
                return inst
            tATT0e = sch.add(Tn, lambda e: attn_grp(e, 0, PB5, 0),
                             deps=[tKFM, tQFM] + pw["PB5"], pe_fence=True)
            tATT0 = sch.add(Tn, lambda e: attn_grp(e, 0, PB5, 1),
                            deps=[tATT0e], pe_fence=True)
            pw["PB5"] = []
            tATT1e = sch.add(Tn, lambda e: attn_grp(e, 1, PB2, 0),
                             deps=[tKFM, tQFM, tZA] + pw["PB2"], pe_fence=True)
            tATT1 = sch.add(Tn, lambda e: attn_grp(e, 1, PB2, 1),
                            deps=[tATT1e], pe_fence=True)
            pw["PB2"] = []
            reads(KFM[par], tATT0, tATT1)

            def mem_grp(e, p=par, par_sel=0):
                inst = None
                for h in range(par_sel, 8, 2):
                    bp = 64 * (h % 2)
                    inst = e.matmul(PB0[:, 64 * h:64 * (h + 1)],
                                    QFM[p][bp:bp + 64,
                                           128 * (h // 2):128 * (h // 2) + 128],
                                    SBF[bp:bp + 64, 64 * (h // 2):64 * (h // 2) + 64],
                                    start=True, stop=True)
                return inst
            tMEMe = sch.add(Tn, lambda e, f=mem_grp: f(e, par_sel=0),
                            deps=[tQFM, tSBF_prev] + pw["PB0"], pe_fence=True)
            tMEM = sch.add(Tn, lambda e, f=mem_grp: f(e, par_sel=1),
                           deps=[tMEMe], pe_fence=True)
            pw["PB0"] = []
            reads(QFM[par], tATT0, tATT1, tMEM)

            tTB = sch.add(Tn, lambda e, p=par: e.transpose(
                PTRB[0:24, 0:128], PACKB[p][:, :], IDENT[:, :]),
                deps=[tPB_] + pw["PTRB"], pe_fence=True)
            reads(PACKB[par], tTB)
            tRVT = sch.add(V, lambda e, p=par: e.tensor_copy(
                RVT[p][:, :], PTRB[0:24, 0:128]), deps=[tTB] + war(RVT, par))
            pw["PTRB"].append(tRVT)


            tECV = sch.add(Sc, lambda e, p=par: e.activation(
                ECV[p][:, :], CV[p][:, :], AF.Exp),
                deps=[tTY3] + war(ECV, par))
            reads(CV[par], tECV)
            tVB1 = sch.add(V, lambda e, p=par: e.tensor_mul(
                VBT[p][:, :], VS[p][:, :], BETA[p][:, :]),
                deps=[tVS, tBE] + war(VBT, par))
            tVB = sch.add(V, lambda e, p=par: e.tensor_tensor(
                VB[p][:, :].rearrange("p (h d) -> p h d", h=8),
                VBT[p][:, :].rearrange("p (h d) -> p h d", h=8),
                ECV[p][:, :].unsqueeze(2).broadcast_to([128, 8, 64]),
                ALU.mult), deps=[tVB1, tECV] + war(VB, par))
            reads(VBT[par], tVB)
            reads(ECV[par], tVB)
            tVB2 = sch.add(V, lambda e, p=par: e.tensor_tensor(
                VB2[p][:, :].rearrange("p (h d) -> p h d", h=8),
                VS[p][:, :].rearrange("p (h d) -> p h d", h=8),
                BMR[p][:, :].unsqueeze(2).broadcast_to([128, 8, 64]),
                ALU.mult), deps=[tVS, tTY3] + war(VB2, par))
            reads(VS[par], tVB, tVB2)
            reads(BETA[par], tVB, tBM8)
            reads(BMR[par], tVB2)

            def t0_mms(e, half, bank, p=par):
                inst = None
                for hh_ in range(4):
                    h = 4 * half + hh_
                    e.matmul(bank[:, 128 * hh_:128 * (hh_ + 1)],
                             SEL24[:, 128 * h:128 * (h + 1)], RVT[p][:, :],
                             start=True, stop=False)
                    inst = e.matmul(bank[:, 128 * hh_:128 * (hh_ + 1)],
                                    IDENT[:, :], MASKN4[:, 0:128],
                                    start=False, stop=True)
                return inst
            tT00 = sch.add(Tn, lambda e: t0_mms(e, 0, PB7),
                           deps=[tRVT, tMSB, tAR, tAE] + pw["PB7"])
            pw["PB7"] = []
            tT01 = sch.add(Tn, lambda e: t0_mms(e, 1, PB3),
                           deps=[tRVT] + pw["PB3"])
            pw["PB3"] = []
            reads(RVT[par], tT00, tT01)

            def e_acts(e, half, bank, p=par):
                return e.activation(
                    ESB[p][:, 512 * half:512 * (half + 1)],
                    bank[:, 0:512], AF.Exp)
            tEACT0 = sch.add(Sc, lambda e: e_acts(e, 0, PB7),
                             deps=[tT00, tTY3] + war(ESB, par))
            pw["PB7"].append(tEACT0)
            tEACT1 = sch.add(Sc, lambda e: e_acts(e, 1, PB3),
                             deps=[tT01, tTY3])
            pw["PB3"].append(tEACT1)
            tAPL0 = sch.add(V, lambda e, p=par: e.tensor_mul(
                ATS[p][:, 0:512], PB5[:, :], ESB[p][:, 0:512]),
                deps=[tATT0, tEACT0] + war(ATS, par))
            pw["PB5"].append(tAPL0)
            tAPL1 = sch.add(V, lambda e, p=par: e.tensor_mul(
                ATS[p][:, 512:1024], PB2[:, :], ESB[p][:, 512:1024]),
                deps=[tATT1, tEACT1])
            pw["PB2"].append(tAPL1)
            reads(ESB[par], tAPL0, tAPL1)

            def del_grp(e, p=par, par_sel=0):
                inst = None
                for h in range(par_sel, 8, 2):
                    bp = 64 * (h % 2)
                    inst = e.matmul(PB1[bp:bp + 64,
                                        64 * (h // 2):64 * (h // 2) + 64],
                                    KTM[p][:, 64 * h:64 * (h + 1)],
                                    VB2[p][:, 64 * h:64 * (h + 1)],
                                    start=True, stop=True,
                                    tile_position=(0, bp))
                return inst
            tDELe = sch.add(Tn, lambda e, f=del_grp: f(e, par_sel=0),
                            deps=[tKTM, tVB2] + pw["PB1"], pe_fence=True)
            tDEL = sch.add(Tn, lambda e, f=del_grp: f(e, par_sel=1),
                           deps=[tDELe], pe_fence=True)
            pw["PB1"] = []
            reads(KTM[par], tDEL)
            reads(VB2[par], tDEL)

            def yin_mms(e, half, p=par):
                inst = None
                for h in range(4 * half, 4 * half + 4):
                    inst = e.matmul(PB5[:, 64 * h:64 * (h + 1)],
                                    ATS[p][:, 128 * h:128 * (h + 1)],
                                    VB[p][:, 64 * h:64 * (h + 1)],
                                    start=True, stop=True)
                return inst
            tYIN0 = sch.add(Tn, lambda e: yin_mms(e, 0),
                            deps=[tAPL0, tVB] + pw["PB5"])
            pw["PB5"] = []
            tYIN1 = sch.add(Tn, lambda e: yin_mms(e, 1), deps=[tAPL1])
            reads(ATS[par], tYIN0, tYIN1)
            reads(VB[par], tYIN0, tYIN1)

            tO1 = sch.add(V, lambda e, p=par: e.tensor_tensor(
                O1[p][:, :].rearrange("p (h d) -> p h d", h=8),
                PB0[:, :].rearrange("p (h d) -> p h d", h=8),
                CR[p][:, :].unsqueeze(2).broadcast_to([128, 8, 64]),
                ALU.mult), deps=[tMEM, tTY3] + war(O1, par))
            pw["PB0"].append(tO1)
            reads(CR[par], tO1)
            tY2 = sch.add(V, lambda e, p=par: e.tensor_add(
                Y2[p][:, :], O1[p][:, :], PB5[:, :]),
                deps=[tO1, tYIN0, tYIN1] + war(Y2, par))
            pw["PB5"].append(tY2)
            reads(O1[par], tY2)
            tSS = sch.add(V, lambda e, p=par: e.tensor_tensor(
                S[:, :].rearrange("p (h d) -> p h d", h=4),
                S[:, :].rearrange("p (h d) -> p h d", h=4),
                ASB[p][:, :].unsqueeze(2).broadcast_to([128, 4, 64]),
                ALU.mult), deps=[tAE, tS_prev])
            reads(ASB[par], tSS)
            tSA = sch.add(V, lambda e: e.tensor_add(
                S[:, :], S[:, :], PB1[:, 0:256]), deps=[tSS, tDEL])
            pw["PB1"].append(tSA)
            tS_prev = tSA
            tSBF = sch.add(V, lambda e: e.tensor_copy(SBF[:, :], S[:, :]),
                           deps=[tSA, tMEM])
            tSBF_prev = tSBF

            reads(STAT[p3], tVS, tEB, tZA, tTY3, tE2)
            tT3 = sch.add(Gp, lambda e, p=par: e.tensor_add(
                T3[p][:, :], E2[p][:, :], Y2[p][:, :]),
                deps=[tE2, tY2] + war(T3, par))
            reads(E2[par], tT3)
            reads(Y2[par], tT3)
            tOB = sch.add(Gp, lambda e, p=par, x=p4: e.tensor_add(
                OUTB[p][:, :], T3[p][:, :], XT[x][:, 0:FS]),
                deps=[tT3, tXT] + war(OUTB, par))
            reads(T3[par], tOB)
            reads(XT[p4], tOB)
            tOD = sch.add(Sy, lambda e, c=cidx, p=par: e.dma_start(
                out_ch[c], OUTB[p][:, :]), deps=[tOB], dma=True,
                sem=f"dout{par}")
            reads(OUTB[par], tOD)
            tOUT_dma2 = tOUT_dma
            tOUT_dma = tOD

        for c in range(min(4, nch)):
            dma(c)
        SD = {}
        for c in range(min(2, nch)):
            SD[c] = stats(c)
            fixtrail(c, SD[c])
        for c in range(nch):
            if c + 2 < nch:
                SD[c + 2] = stats(c + 2)
            main(c, SD.pop(c))
            if c + 2 < nch:
                fixtrail(c + 2, SD[c + 2])
            if c + 4 < nch:
                dma(c + 4)

        sch.add(Sy, lambda e: e.nop(),
                deps=[tOUT_dma, tOUT_dma2, tQR, tROWD, tFIXDONE])

        sch.resolve()

        @block.sync
        def _(eng):
            sch.emit_one(Sy, eng, sems)

        @block.gpsimd
        def _(eng):
            sch.emit_one(Gp, eng, sems)

        @block.vector
        def _(eng):
            sch.emit_one(V, eng, sems)

        @block.scalar
        def _(eng):
            sch.emit_one(Sc, eng, sems)

        @block.tensor
        def _(eng):
            sch.emit_one(Tn, eng, sems)

    return nc


_NC_CACHE = None


def kernel(L0_attn_x, L0_attn_Wq, L0_attn_bq, L0_attn_Wk, L0_attn_bk,
           L0_attn_Wv, L0_attn_bv, L0_attn_Wbeta, L0_attn_bbeta,
           L0_attn_Walpha, L0_attn_balpha, L0_ffn_rhs):
    global _NC_CACHE
    x = np.ascontiguousarray(np.asarray(L0_attn_x, dtype=np.float32))
    Ws = {"q": np.asarray(L0_attn_Wq, np.float32),
          "k": np.asarray(L0_attn_Wk, np.float32),
          "v": np.asarray(L0_attn_Wv, np.float32),
          "be": np.asarray(L0_attn_Wbeta, np.float32),
          "al": np.asarray(L0_attn_Walpha, np.float32)}
    Bs = {"q": np.asarray(L0_attn_bq, np.float32),
          "k": np.asarray(L0_attn_bk, np.float32),
          "v": np.asarray(L0_attn_bv, np.float32),
          "be": np.asarray(L0_attn_bbeta, np.float32),
          "al": np.asarray(L0_attn_balpha, np.float32)}
    rhs = np.asarray(L0_ffn_rhs, np.float32)

    if _NC_CACHE is None:
        _NC_CACHE = build_program()
    nc = _NC_CACHE

    in_maps = []
    for core in range(8):
        b = core // 2
        hh = core % 2
        F0 = FS * hh
        perm = np.r_[F0:F0 + FS, 0:F0, F0 + FS:D]
        xp = np.ascontiguousarray(x[b][:, perm])
        m = {
            "xt": xp,
            "xf": np.ascontiguousarray(xp.T),
            "rhs": np.ascontiguousarray(rhs[perm][:, F0:F0 + FS]),
        }
        for p in PROJ:
            m[f"w{p}"] = np.ascontiguousarray(Ws[p][F0:F0 + FS, :][:, perm].T)
            m[f"b{p}"] = np.ascontiguousarray(Bs[p][F0:F0 + FS]).reshape(1, FS)
        in_maps.append(m)

    res = run_bass_kernel_spmd(nc, in_maps, core_ids=list(range(8)))
    out = np.empty((B, T, D), np.float32)
    for core in range(8):
        b = core // 2
        hh = core % 2
        F0 = FS * hh
        out[b, :, F0:F0 + FS] = np.asarray(res.results[core]["out"])
    return out


# version 28 (final)
# speedup vs baseline: 1.0460x; 1.0460x over previous
import numpy as np
import ml_dtypes
import concourse.bass as bass
import concourse.mybir as mybir
from concourse.bass_utils import run_bass_kernel_spmd
from contextlib import ExitStack

F32 = mybir.dt.float32
BF16 = mybir.dt.bfloat16
AF = mybir.ActivationFunctionType
ALU = mybir.AluOpType
AX = mybir.AxisListType

B, T, D = 4, 4096, 1024
C = 128
NCH = T // C
FS = 512
MAGIC = 12582912.0
LN8 = 2.0794415416798357
EPS = 1e-5
PROJ = ("q", "k", "v", "be", "al")


class Tok:
    __slots__ = ("sem", "val", "dma")
    def __init__(self, dma=False):
        self.sem = None
        self.val = None
        self.dma = dma


class Sched:
    def __init__(self):
        self.recs = []

    def add(self, eng, fn, deps=(), dma=False, sem=None, dma_n=1,
            pe_fence=False):
        t = Tok(dma)
        flat = []
        for d in deps:
            if d is None:
                continue
            if isinstance(d, (list, tuple)):
                flat.extend(x for x in d if x is not None)
            else:
                flat.append(d)
        self.recs.append((eng, fn, flat, t, dma, sem if dma else eng, dma_n,
                          pe_fence))
        return t

    def resolve(self):
        counters = {}
        for eng, fn, deps, t, dma, sem, dma_n, pe_fence in self.recs:
            inc = 16 * dma_n if dma else 1
            counters[sem] = counters.get(sem, 0) + inc
            t.sem = sem
            t.val = counters[sem]

    def emit_one(self, name, engine, sems):
        seen = {}
        fence = None
        for eng, fn, deps, t, dma, sem, dma_n, pe_fence in self.recs:
            if eng != name:
                continue
            if fence is not None:
                engine.wait_ge(sems[name], fence.val)
                fence = None
            if pe_fence:
                fence = t
            need_drain = False
            for d in deps:
                if d.sem == name and not d.dma:
                    need_drain = True
                    continue
                if seen.get(d.sem, 0) >= d.val:
                    continue
                engine.wait_ge(sems[d.sem], d.val)
                seen[d.sem] = d.val
            if need_drain and name in ("vector", "scalar"):
                engine.drain()
            out = fn(engine)
            insts = out if isinstance(out, (list, tuple)) else [out]
            for inst in insts:
                inst.then_inc(sems[sem], 16 if dma else 1)


def _consts():
    c = {}
    c["ident_bf"] = np.eye(128, dtype=ml_dtypes.bfloat16)
    c["ident_f"] = np.eye(128, dtype=np.float32)
    ji = np.arange(128)
    mn = np.zeros((128, 128), np.float32)
    mn[ji[:, None] > ji[None, :]] = -1e9
    c["maskneg4_bf"] = np.tile(mn, (1, 4)).astype(ml_dtypes.bfloat16)
    c["ut64_f"] = np.triu(np.ones((128, 128), np.float32)) / 64.0
    c["msumw_f"] = np.full((128, 1), 1.0 / 64.0, np.float32)
    sel = np.zeros((24, 8 * 128), np.float32)
    for h in range(8):
        for k in (h, 8 + h, 16 + h):
            sel[k, 128 * h:128 * (h + 1)] = 1.0
    c["sel24_bf"] = sel.astype(ml_dtypes.bfloat16)
    c["ones1_f"] = np.ones((1, 128), np.float32)
    c["onesc_bf"] = np.ones((128, 1), dtype=ml_dtypes.bfloat16)
    c["cimm_f"] = np.tile(np.array([[EPS, -0.5, 0.5, -1.0]], np.float32),
                          (128, 1))
    return c


def build_program(nch=NCH, stop_after=None):
    import os as _os0
    nc = bass.Bass("TRN2", target_bir_lowering=False,
                   debug=bool(_os0.environ.get("DBGPTRB")))

    xt_d = nc.dram_tensor("xt", [T, D], F32, kind="ExternalInput").ap()
    xf_d = nc.dram_tensor("xf", [D, T], F32, kind="ExternalInput").ap()
    w_d = {p: nc.dram_tensor(f"w{p}", [D, FS], F32, kind="ExternalInput").ap()
           for p in PROJ}
    b_d = {p: nc.dram_tensor(f"b{p}", [1, FS], F32, kind="ExternalInput").ap()
           for p in PROJ}
    rhs_d = nc.dram_tensor("rhs", [D, FS], F32, kind="ExternalInput").ap()
    out_d = nc.dram_tensor("out", [T, FS], F32, kind="ExternalOutput").ap()

    cn = _consts()
    ct = {k: nc.inline_tensor(v, name=k) for k, v in cn.items()}

    sch = Sched()
    V, Sc, Tn, Sy, Gp = "vector", "scalar", "tensor", "sync", "gp"

    with ExitStack() as ctx:
        def sb(name, shape, dt=F32):
            return ctx.enter_context(nc.sbuf_tensor(name, shape, dt))

        IDENT = sb("IDENT", [128, 128], BF16)
        IDENTF = sb("IDENTF", [128, 128], F32)
        MASKN4 = sb("MASKN4", [128, 512], BF16)
        UT64 = sb("UT64", [128, 128], F32)
        MSUMW = sb("MSUMW", [128, 1], F32)
        SEL24 = sb("SEL24", [24, 1024], BF16)
        ONES1 = sb("ONES1", [1, 128], F32)
        ONESC = sb("ONESC", [128, 1], BF16)
        CIMM = sb("CIMM", [128, 4], F32)
        for i, v in enumerate([EPS, -0.5, 0.5, -1.0]):
            nc.const_aps.register(F32, v, CIMM[:, i:i + 1])

        WT = {p: sb(f"WT{p}", [128, 8 * FS], BF16) for p in PROJ}
        QRHS = sb("QRHS", [128, 8 * FS], BF16)
        FIXR = {p: sb(f"FIXR{p}", [4, FS], BF16) for p in PROJ}
        BSTG = sb("BSTG", [1, 3 * FS], F32)
        BHBF = sb("BHBF", [1, 2 * FS], BF16)
        RS4 = sb("RS4", [128, 8], F32)
        RSB = sb("RSB", [128, FS], F32)
        RROW = sb("RROW", [1, 2 * FS], F32)

        S = sb("S", [128, 4 * 64], F32)
        SBF = sb("SBF", [128, 4 * 64], BF16)

        def ps(name, shape):
            return ctx.enter_context(nc.psum_tensor(name, shape, F32))
        PB0 = ps("PB0", [128, 512])
        PB1 = ps("PB1", [128, 512])
        PB2 = ps("PB2", [128, 512])
        PB3 = ps("PB3", [128, 512])
        PB4 = ps("PB4", [128, 512])
        PB5 = ps("PB5", [128, 512])
        PB7 = ps("PB7", [128, 512])
        PTRB = ctx.enter_context(nc.psum_tensor("PTRB", [128, 1024], BF16))

        sems = {}
        for name in (Sy, Gp, V, Sc, Tn, "dconst", "dbias", "drow",
                     "dgp0", "dgp1", "din0", "din1", "din2", "din3",
                     "dout0", "dout1"):
            sems[name] = ctx.enter_context(nc.semaphore(f"sem_{name}"))
        block = ctx.enter_context(nc.Block())

        const_list = (("ident_bf", IDENT), ("ident_f", IDENTF),
                      ("maskneg4_bf", MASKN4), ("ut64_f", UT64),
                      ("msumw_f", MSUMW), ("sel24_bf", SEL24),
                      ("ones1_f", ONES1), ("onesc_bf", ONESC),
                      ("cimm_f", CIMM))
        def const_dmas(e):
            return [e.dma_start(t[:, :], ct[n].ap()) for n, t in const_list]
        tCONST = sch.add(Sy, const_dmas, dma=True, sem="dconst",
                         dma_n=len(const_list))

        tS0 = sch.add(V, lambda e: e.memset(S[:, :], 0.0))
        tSB0 = sch.add(V, lambda e: e.memset(SBF[:, :], 0.0))

        pb7_readers = []

        with ExitStack() as sctx:
            WSTG = [sctx.enter_context(nc.sbuf_tensor(f"WSTG{i}", [128, 8 * FS], F32))
                    for i in range(2)]
            QSC = sctx.enter_context(nc.sbuf_tensor("QSC", [128, 1024], F32))

            tWT = {}
            stg_free = [None, None]
            for i, p in enumerate(PROJ):
                stg = WSTG[i % 2]
                tD = sch.add(Gp, lambda e, p=p, stg=stg: e.dma_start(
                    stg[:, :].rearrange("p (c f) -> p c f", c=8),
                    w_d[p].rearrange("(c p) f -> p c f", p=128)),
                    deps=[stg_free[i % 2]], dma=True, sem=f"dgp{i % 2}")
                tC = sch.add(V, lambda e, p=p, stg=stg: e.tensor_copy(
                    WT[p][:, :], stg[:, :]), deps=[tD])
                tWT[p] = tC
                stg_free[i % 2] = tC
            tRD = sch.add(Gp, lambda e: e.dma_start(
                WSTG[1][:, :].rearrange("p (c f) -> p c f", c=8),
                rhs_d.rearrange("(c p) f -> p c f", p=128)),
                deps=[stg_free[1]], dma=True, sem="dgp1")

            tFIXR = {}
            tfix_parts = []
            prevb = None
            for p in PROJ:
                t1 = sch.add(Sy, lambda e, p=p: e.dma_start(
                    BSTG[0:1, 0:FS], b_d[p]), deps=[prevb], dma=True,
                    sem="dbias")
                t_h = sch.add(V, lambda e, p=p: e.tensor_copy(
                    BHBF[0:1, 0:FS], BSTG[0:1, 0:FS]), deps=[t1])
                t_b = sch.add(V, lambda e, p=p: e.tensor_copy(
                    BSTG[0:1, FS:2 * FS], BHBF[0:1, 0:FS]), deps=[t_h])
                t_l = sch.add(V, lambda e, p=p: e.tensor_sub(
                    BSTG[0:1, 2 * FS:3 * FS], BSTG[0:1, 0:FS],
                    BSTG[0:1, FS:2 * FS]), deps=[t_b])
                t_lb = sch.add(V, lambda e, p=p: e.tensor_copy(
                    BHBF[0:1, FS:2 * FS], BSTG[0:1, 2 * FS:3 * FS]),
                    deps=[t_l])
                tD1 = sch.add(Sy, lambda e, p=p: e.dma_start(
                    FIXR[p][1:2, :], BHBF[0:1, 0:FS]), deps=[t_h], dma=True,
                    sem="dbias")
                tD3 = sch.add(Sy, lambda e, p=p: e.dma_start(
                    FIXR[p][3:4, :], BHBF[0:1, 0:FS]), deps=[t_h], dma=True,
                    sem="dbias")
                tD2 = sch.add(Sy, lambda e, p=p: e.dma_start(
                    FIXR[p][2:3, :], BHBF[0:1, FS:2 * FS]),
                    deps=[t_lb], dma=True, sem="dbias")
                t_c2 = sch.add(V, lambda e: e.engine_nop(),
                               deps=[tD2, tD1, tD3])
                def sw_mms(e, p=p):
                    inst = None
                    for dc in range(8):
                        inst = e.matmul(PB7[0:1, 0:FS], ONESC[:, :],
                                        WT[p][:, 512 * dc:512 * (dc + 1)],
                                        start=(dc == 0), stop=(dc == 7))
                    return inst
                t_sw = sch.add(Tn, sw_mms, deps=[tWT[p], tCONST] + pb7_readers)
                t_swe = sch.add(V, lambda e, p=p: e.tensor_copy(
                    FIXR[p][0:1, :], PB7[0:1, 0:FS]), deps=[t_sw])
                pb7_readers = [t_swe]
                tFIXR[p] = [t_swe]
                tfix_parts.extend([tD2, t_swe])
                prevb = t_c2

            tFIXDONE = sch.add(V, lambda e: e.engine_nop(),
                               deps=tfix_parts)
            for p in PROJ:
                tFIXR[p] = [tFIXDONE]

            QTB = sctx.enter_context(nc.sbuf_tensor("QTB", [128, 4096], BF16))
            w0r = WSTG[0][:, :].rearrange("p (ft x) -> p ft x", ft=4)
            def rtr_f(e, dc):
                inst = None
                for ft in range(4):
                    inst = e.transpose(
                        PB7[:, 128 * ft:128 * (ft + 1)],
                        WSTG[1][:, 512 * dc + 128 * ft:512 * dc + 128 * (ft + 1)],
                        IDENTF[:, :])
                return inst
            tprev = [tRD, tWT["al"], tCONST] + pb7_readers
            for dc in range(8):
                tT = sch.add(Tn, lambda e, dc=dc: rtr_f(e, dc), deps=tprev, pe_fence=True)
                tE = sch.add(V, lambda e, dc=dc: e.tensor_copy(
                    w0r[:, :, 128 * dc:128 * (dc + 1)],
                    PB7[:, 0:512].rearrange("p (ft x) -> p ft x", ft=4)),
                    deps=[tT])
                tprev = [tE]
            tRT = tprev[0]
            pb7_readers = [tRT]
            def rs_f(e):
                for ft in range(4):
                    e.reduce_max(RS4[:, ft:ft + 1],
                                 WSTG[0][:, 1024 * ft:1024 * (ft + 1)],
                                 axis=AX.X, apply_absolute_value=True)
                e.drain()
                e.tensor_scalar(RS4[:, 0:4], RS4[:, 0:4], 1e-12, 1.0 / 127.0,
                                ALU.max, ALU.mult)
                e.drain()
                return e.reciprocal(RS4[:, 4:8], RS4[:, 0:4])
            tRS = sch.add(V, rs_f, deps=[tRT])
            def q_f(e):
                inst = None
                for ft in range(4):
                    e.drain()
                    e.tensor_scalar(WSTG[1][:, 0:1024],
                                    WSTG[0][:, 1024 * ft:1024 * (ft + 1)],
                                    RS4[:, 4 + ft:5 + ft], None, ALU.mult)
                    e.drain()
                    inst = e.tensor_scalar(QTB[:, 1024 * ft:1024 * (ft + 1)],
                                           WSTG[1][:, 0:1024], MAGIC, -MAGIC,
                                           ALU.add, ALU.add)
                return inst
            tQF = sch.add(V, q_f, deps=[tRS])
            tprev = [tQF, tRT]
            for dc in range(8):
                def btr_f(e, dc=dc):
                    inst = None
                    for ft in range(4):
                        inst = e.transpose(
                            PTRB[:, 128 * ft:128 * (ft + 1)],
                            QTB[:, 1024 * ft + 128 * dc:1024 * ft + 128 * (dc + 1)],
                            IDENT[:, :])
                    return inst
                tT = sch.add(Tn, btr_f, deps=tprev, pe_fence=True)
                tE = sch.add(V, lambda e, dc=dc: e.tensor_copy(
                    QRHS[:, 512 * dc:512 * (dc + 1)],
                    PTRB[:, 0:512]), deps=[tT])
                tprev = [tE]
            tQBACK = tprev[0]
            pb7_readers = [tQBACK]
            RSRT = sctx.enter_context(nc.sbuf_tensor("RSRT", [4, 128], F32))
            tTR = sch.add(Tn, lambda e: e.transpose(
                PB7[0:4, 0:128], RS4[:, 0:4], IDENTF[:, :]),
                deps=[tRS, tQBACK] + pb7_readers, pe_fence=True)
            tTRE = sch.add(V, lambda e: e.tensor_copy(
                RSRT[:, :], PB7[0:4, 0:128]), deps=[tTR])
            pb7_readers = [tTRE]
            tROWD = sch.add(Sy, lambda e: e.dma_start(
                RROW[0:1, 0:512].rearrange("o (a b) -> o a b", a=4),
                RSRT[:, :]), deps=[tTRE], dma=True, sem="drow")
            tBC1 = sch.add(Tn, lambda e: e.matmul(
                PB7[:, 0:FS], ONES1[:, :], RROW[0:1, 0:FS],
                start=True, stop=True), deps=[tROWD, tCONST] + pb7_readers)
            tRSB = sch.add(V, lambda e: e.tensor_copy(RSB[:, :], PB7[:, 0:FS]),
                           deps=[tBC1])
            pb7_readers = [tRSB]
            tQR = sch.add(V, lambda e: e.engine_nop(), deps=[tQBACK, tRSB])

        def db(name, shape, dt=F32, n=2):
            return [sb(f"{name}{i}", shape, dt) for i in range(n)]
        XT = db("XT", [128, D], n=4)
        XF = db("XF", [128, D], n=4)
        XFB = db("XFB", [128, D], BF16, n=3)
        STAT = db("STAT", [128, 16], n=3)
        BN6 = db("BN6", [128, 12], n=3)
        SCR8 = db("SCR8", [128, 8])
        PACKA = db("PACKA", [128, 4], BF16, n=3)
        FIXL = db("FIXL", [4, 128], BF16, n=3)
        QT2 = db("QT2", [128, D])
        QLR = db("QLR", [128, D], BF16, n=3)
        QLF = db("QLF", [128, D], BF16)
        VS = db("VS", [128, FS])
        BETA = db("BETA", [128, FS])
        EB = db("EB", [128, FS])
        ZA = db("ZA", [128, FS])
        ZC = db("ZC", [128, FS])
        EA = db("EA", [128, FS])
        PML = db("PML", [128, FS])
        PM8 = db("PM8", [128, 8])
        BM8 = db("BM8", [128, 8])
        MSB = db("MSB", [128, 8])
        CURVE = db("CURVE", [128, 8])
        CR = db("CR", [128, 8])
        BMR = db("BMR", [128, 8])
        RV = db("RV", [128, 8])
        CV = db("CV", [128, 8])
        PACKB = db("PACKB", [128, 24], BF16)
        RVT = db("RVT", [24, 128], BF16)
        AROW = db("AROW", [1, 8])
        ASB = db("ASB", [128, 4])
        QFM = db("QFM", [128, FS], BF16)
        KFM = db("KFM", [128, FS], BF16)
        KTM = db("KTM", [128, FS], BF16)
        VB = db("VB", [128, FS], BF16)
        VBT = db("VBT", [128, FS])
        ECV = db("ECV", [128, 8])
        VB2 = db("VB2", [128, FS], BF16)
        ESB = db("ESB", [128, 8 * 128])
        ATS = db("ATS", [128, 8 * 128], BF16)
        O1 = db("O1", [128, FS])
        Y2 = db("Y2", [128, FS])
        E2 = db("E2", [128, FS])
        T3 = db("T3", [128, FS])
        OUTB = db("OUTB", [128, FS])

        xt_ch = xt_d.rearrange("(n p) d -> n p d", p=128)
        out_ch = out_d.rearrange("(n p) f -> n p f", p=128)

        last_use = {}

        def reads(buf_slot, *toks):
            last_use.setdefault(id(buf_slot), []).extend(
                t for t in toks if t is not None)

        def war(buf_list, par):
            return last_use.pop(id(buf_list[par]), [])

        pw = {"PB0": [], "PB1": [], "PB2": [], "PB3": [], "PB4": [],
              "PB5": [], "PB7": pb7_readers, "PTRB": []}

        tS_prev = tS0
        tSBF_prev = tSB0
        tOUT_dma = None
        tOUT_dma2 = None

        tINs = {}

        def dma(cidx):
            p4 = cidx % 4
            startup_gate = [tQR, tROWD] if cidx < 4 else []

            def in_dmas(e, c=cidx, p=p4):
                return [
                    e.dma_start(XT[p][:, :], xt_ch[c]),
                    e.dma_start(
                        XF[p][:, :].rearrange("p (c t) -> p c t", c=8),
                        xf_d.rearrange("(c p) t -> p c t", p=128)
                        [:, :, 128 * c:128 * (c + 1)]),
                ]
            tINs[cidx] = sch.add(
                Sy, in_dmas, deps=war(XT, p4) + war(XF, p4) + startup_gate,
                dma=True, sem=f"din{p4}", dma_n=2)

        def stats(cidx):
            par = cidx % 3
            p4 = cidx % 4
            tXT = tXF = tINs[cidx]

            def bn_f(e, p=par, x=p4):
                e.bn_stats(BN6[p][:, 0:6], XT[x][:, 0:512])
                return e.bn_stats(BN6[p][:, 6:12], XT[x][:, 512:1024])
            tBN = sch.add(V, bn_f, deps=[tXT] + war(BN6, par))
            tAG = sch.add(V, lambda e, p=par: e.bn_aggr(
                STAT[p][:, 0:2], BN6[p][:, :]), deps=[tBN] + war(STAT, par))
            tMX = sch.add(V, lambda e, p=par, x=p4: e.reduce_max(
                STAT[p][:, 6:7], XT[x][:, :], axis=AX.X), deps=[tXT, tAG])
            tMN = sch.add(V, lambda e, p=par, x=p4: e.tensor_reduce(
                STAT[p][:, 7:8], XT[x][:, :], axis=AX.X, op=ALU.min),
                deps=[tXT, tAG])
            reads(XT[p4], tBN, tMX, tMN)

            tLN_ = sch.add(Sc, lambda e, p=par: e.activation(
                STAT[p][:, 2:3], STAT[p][:, 1:2], AF.Ln, bias=EPS), deps=[tAG])
            tR_ = sch.add(Sc, lambda e, p=par: e.activation(
                STAT[p][:, 3:4], STAT[p][:, 2:3], AF.Exp, scale=-0.5), deps=[tLN_])
            tSQ_ = sch.add(Sc, lambda e, p=par: e.activation(
                STAT[p][:, 4:5], STAT[p][:, 2:3], AF.Exp, scale=0.5), deps=[tLN_])

            def tiny1(e, p=par):
                st = STAT[p]
                e.tensor_scalar(st[:, 5:6], st[:, 2:3], -0.5, -LN8,
                                ALU.mult, ALU.add)
                e.tensor_sub(st[:, 8:9], st[:, 6:7], st[:, 0:1])
                e.tensor_sub(st[:, 9:10], st[:, 0:1], st[:, 7:8])
                e.drain()
                e.tensor_max(st[:, 8:9], st[:, 8:9], st[:, 9:10])
                e.drain()
                e.tensor_mul(st[:, 8:9], st[:, 8:9], st[:, 3:4])
                e.drain()
                e.tensor_scalar(st[:, 9:10], st[:, 8:9], 1e-12, 1.0 / 127.0,
                                ALU.max, ALU.mult)
                e.drain()
                return e.reciprocal(st[:, 10:11], st[:, 9:10])
            tTY1 = sch.add(V, tiny1, deps=[tMX, tMN, tR_])
            def tiny2(e, p=par):
                st = STAT[p]
                e.tensor_mul(st[:, 11:12], st[:, 3:4], st[:, 10:11])
                e.drain()
                e.tensor_mul(st[:, 12:13], st[:, 0:1], st[:, 11:12])
                e.drain()
                return e.tensor_scalar(st[:, 12:13], st[:, 12:13], -1.0, None,
                                       ALU.mult)
            tTY2 = sch.add(V, tiny2, deps=[tTY1])

            def packa_f(e, p=par):
                st = STAT[p]; pa = PACKA[p]
                e.tensor_scalar(pa[:, 0:1], st[:, 0:1], -1.0, None, ALU.mult)
                e.tensor_copy(pa[:, 1:2], st[:, 4:5])
                e.drain()
                e.tensor_copy(pa[:, 2:3], pa[:, 1:2])
                return e.tensor_sub(pa[:, 3:4], st[:, 4:5], pa[:, 1:2])
            tPA = sch.add(V, packa_f,
                          deps=[tSQ_, tTY2] + war(PACKA, par))
            reads(STAT[par], tLN_, tR_, tSQ_, tTY1, tTY2, tPA)

            tXFB = sch.add(Sc, lambda e, p=par, x=p4: e.activation(
                XFB[p][:, :], XF[x][:, :], AF.Copy),
                deps=[tXF] + war(XFB, par))
            reads(XF[p4], tXFB)

            p2 = cidx % 2
            tQT2 = sch.add(Sc, lambda e, p=p2, s=par, x=p4: e.activation(
                QT2[p][:, :], XT[x][:, :], AF.Identity, bias=STAT[s][:, 12:13],
                scale=STAT[s][:, 11:12]), deps=[tXT, tTY2] + war(QT2, p2))
            reads(XT[p4], tQT2)
            reads(STAT[par], tQT2)
            tQLR = sch.add(V, lambda e, p=par, q=p2: e.tensor_scalar(
                QLR[p][:, :], QT2[q][:, :], MAGIC, -MAGIC, ALU.add, ALU.add),
                deps=[tQT2] + war(QLR, par))
            reads(QT2[p2], tQLR)

            return dict(tXT=tXT, tR_=tR_, tTY1=tTY1, tTY2=tTY2, tPA=tPA,
                        tXFB=tXFB, tQLR=tQLR)

        def fixtrail(cidx, F):
            par = cidx % 3
            tTA = sch.add(Tn, lambda e, p=par: e.transpose(
                PTRB[0:4, 0:128], PACKA[p][:, :], IDENT[:, :]),
                deps=[F["tPA"], tCONST] + pw["PTRB"], pe_fence=True)
            pw["PTRB"] = []
            tFXL = sch.add(V, lambda e, p=par: e.tensor_copy(
                FIXL[p][:, :], PTRB[0:4, 0:128]), deps=[tTA] + war(FIXL, par))
            pw["PTRB"].append(tFXL)
            reads(PACKA[par], tTA)
            F["tFXL"] = tFXL

        def main(cidx, F):
            nonlocal tS_prev, tSBF_prev, tOUT_dma, tOUT_dma2
            par = cidx % 2
            p3 = cidx % 3
            p4 = cidx % 4
            tXT = F["tXT"]; tR_ = F["tR_"]; tTY1 = F["tTY1"]
            tTY2 = F["tTY2"]; tFXL = F["tFXL"]; tXFB = F["tXFB"]
            tm_toks = {}
            for pname, bank in (("al", PB2), ("v", PB0), ("be", PB1)):
                def tm_mms(e, p=p3, pn=pname, bank=bank):
                    for dc in range(8):
                        e.matmul(bank[:, :], XFB[p][:, 128 * dc:128 * (dc + 1)],
                                 WT[pn][:, 512 * dc:512 * (dc + 1)],
                                 start=(dc == 0), stop=False)
                    return e.matmul(bank[:, :], FIXL[p][:, :], FIXR[pn][:, :],
                                    start=False, stop=True)
                nm = "PB" + str(("v", "be", "al").index(pname))
                tm_toks[pname] = sch.add(
                    Tn, tm_mms,
                    deps=[tXFB, tFXL, tWT[pname]] + tFIXR[pname] + pw[nm])
                pw[nm] = []

            fm_toks = {}
            for pname, bank, nm in (("q", PB3, "PB3"), ("k", PB4, "PB4")):
                def fm_mms(e, p=p3, pn=pname, bank=bank):
                    inst = None
                    for fc in range(4):
                        for dc in range(8):
                            e.matmul(bank[:, 128 * fc:128 * (fc + 1)],
                                     WT[pn][:, 512 * dc + 128 * fc:
                                            512 * dc + 128 * (fc + 1)],
                                     XFB[p][:, 128 * dc:128 * (dc + 1)],
                                     start=(dc == 0), stop=False)
                        inst = e.matmul(bank[:, 128 * fc:128 * (fc + 1)],
                                        FIXR[pn][:, 128 * fc:128 * (fc + 1)],
                                        FIXL[p][:, :], start=False, stop=True)
                    return inst
                fm_toks[pname] = sch.add(
                    Tn, fm_mms,
                    deps=[tXFB, tFXL, tWT[pname]] + tFIXR[pname] + pw[nm])
                pw[nm] = []

            tZA = sch.add(Sc, lambda e, p=par, s=p3: e.activation(
                ZA[p][:, :], PB2[:, :], AF.Copy, scale=STAT[s][:, 3:4]),
                deps=[tm_toks["al"], tR_] + war(ZA, par))
            pw["PB2"].append(tZA)
            tVS = sch.add(Sc, lambda e, p=par, s=p3: e.activation(
                VS[p][:, :], PB0[:, :], AF.Copy, scale=STAT[s][:, 3:4]),
                deps=[tm_toks["v"], tR_] + war(VS, par))
            pw["PB0"].append(tVS)
            tZC = sch.add(V, lambda e, p=par: e.tensor_scalar(
                ZC[p][:, :], ZA[p][:, :], 10.0, -10.0, ALU.min, ALU.max),
                deps=[tZA] + war(ZC, par))
            reads(ZA[par], tZC)
            tEA = sch.add(Sc, lambda e, p=par: e.activation(
                EA[p][:, :], ZC[p][:, :], AF.Exp, scale=-1.0),
                deps=[tZC] + war(EA, par))
            reads(ZC[par], tEA)
            tEB = sch.add(Sc, lambda e, p=par, s=p3: e.activation(
                EB[p][:, :], PB1[:, :], AF.Exp, scale=STAT[s][:, 3:4]),
                deps=[tm_toks["be"], tR_] + war(EB, par))
            pw["PB1"].append(tEB)
            tPML = sch.add(Sc, lambda e, p=par: e.activation(
                PML[p][:, :], EA[p][:, :], AF.Ln, bias=1.0),
                deps=[tEA] + war(PML, par))
            reads(EA[par], tPML)

            tBE = sch.add(Sc, lambda e, p=par: e.activation(
                BETA[p][:, :], EB[p][:, :], AF.Ln, bias=1.0),
                deps=[tEB] + war(BETA, par))
            reads(EB[par], tBE)
            tPM8 = sch.add(V, lambda e, p=par: e.reduce_sum(
                PM8[p][:, :], PML[p][:, :].rearrange("p (h d) -> p h d", h=8),
                axis=AX.X), deps=[tPML] + war(PM8, par))
            reads(PML[par], tPM8)
            tBM8 = sch.add(V, lambda e, p=par: e.reduce_sum(
                BM8[p][:, :], BETA[p][:, :].rearrange("p (h d) -> p h d", h=8),
                axis=AX.X), deps=[tBE] + war(BM8, par))
            tQFM = sch.add(Sc, lambda e, p=par: e.activation(
                QFM[p][:, :], PB3[:, :], AF.Copy),
                deps=[fm_toks["q"]] + war(QFM, par))
            pw["PB3"].append(tQFM)
            tKFM = sch.add(Sc, lambda e, p=par: e.activation(
                KFM[p][:, :], PB4[:, :], AF.Copy),
                deps=[fm_toks["k"]] + war(KFM, par))
            pw["PB4"].append(tKFM)

            tQLR = F["tQLR"]
            def ql_tr(e, p=p3):
                e.nop()
                inst = None
                for dc in range(8):
                    inst = e.transpose(
                        PTRB[:, 128 * dc:128 * (dc + 1)],
                        QLR[p][:, 128 * dc:128 * (dc + 1)], IDENT[:, :])
                return inst
            tQTR = sch.add(Tn, ql_tr, deps=[tQLR] + pw["PTRB"], pe_fence=True)
            pw["PTRB"] = []
            tQLF = sch.add(Sc, lambda e, p=par: e.activation(
                QLF[p][:, :], PTRB[:, :], AF.Copy),
                deps=[tQTR] + war(QLF, par))
            pw["PTRB"].append(tQLF)
            reads(QLR[p3], tQTR)

            def ktm_mms(e, p=p3):
                for dc in range(8):
                    e.matmul(PB0[:, :], XFB[p][:, 128 * dc:128 * (dc + 1)],
                             WT["k"][:, 512 * dc:512 * (dc + 1)],
                             start=(dc == 0), stop=False)
                return e.matmul(PB0[:, :], FIXL[p][:, :], FIXR["k"][:, :],
                                start=False, stop=True)
            tKMM = sch.add(Tn, ktm_mms,
                           deps=[tXFB, tFXL, tWT["k"]] + tFIXR["k"] + pw["PB0"])
            pw["PB0"] = []
            tKTM = sch.add(Sc, lambda e, p=par: e.activation(
                KTM[p][:, :], PB0[:, :], AF.Copy),
                deps=[tKMM] + war(KTM, par))
            pw["PB0"].append(tKTM)

            def cums(e, p=par):
                e.matmul(PB7[:, 128:136], UT64[:, :], PM8[p][:, :],
                         start=True, stop=True)
                return e.matmul(PB7[0:1, 136:144], MSUMW[:, :], PM8[p][:, :],
                                start=True, stop=True)
            tM = sch.add(Tn, cums, deps=[tPM8] + pw["PB7"])
            pw["PB7"] = []
            reads(PM8[par], tM)
            tMSB = sch.add(V, lambda e, p=par: e.tensor_copy(
                MSB[p][:, :], PB7[:, 128:136]), deps=[tM] + war(MSB, par))
            tAR = sch.add(Sc, lambda e, p=par: e.activation(
                AROW[p][:, :], PB7[0:1, 136:144], AF.Exp, scale=-1.0),
                deps=[tM] + war(AROW, par))
            pw["PB7"] += [tMSB, tAR]
            tCUR = sch.add(Sc, lambda e, p=par: e.activation(
                CURVE[p][:, :], MSB[p][:, :], AF.Exp, scale=-1.0),
                deps=[tMSB] + war(CURVE, par))
            def tiny3(e, p=par, s=p3):
                e.tensor_scalar(CR[p][:, :], CURVE[p][:, :], STAT[s][:, 3:4],
                                0.125, ALU.mult, ALU.mult)
                e.tensor_scalar(BMR[p][:, :], BM8[p][:, :], STAT[s][:, 3:4],
                                0.125 / 64.0, ALU.mult, ALU.mult)
                e.tensor_scalar(RV[p][:, :], MSB[p][:, :], -1.0, STAT[s][:, 5:6],
                                ALU.mult, ALU.add)
                return e.tensor_scalar(CV[p][:, :], MSB[p][:, :], 1.0,
                                       STAT[s][:, 5:6], ALU.mult, ALU.add)
            tTY3 = sch.add(V, tiny3,
                           deps=[tCUR, tMSB, tBM8, tTY1] + war(CR, par)
                           + war(BMR, par) + war(RV, par) + war(CV, par))
            reads(CURVE[par], tTY3)
            reads(BM8[par], tTY3)
            def packb_f(e, p=par):
                pb = PACKB[p]; rv = RV[p]; sc8 = SCR8[p]
                e.tensor_copy(pb[:, 0:8], rv[:, :])
                e.drain()
                e.tensor_sub(sc8[:, 0:8], rv[:, :], pb[:, 0:8])
                e.drain()
                e.tensor_copy(pb[:, 8:16], sc8[:, 0:8])
                e.drain()
                return e.tensor_sub(pb[:, 16:24], sc8[:, 0:8], pb[:, 8:16])
            tPB_ = sch.add(V, packb_f, deps=[tTY3] + war(PACKB, par))
            reads(RV[par], tPB_)
            reads(SCR8[par], tPB_)
            def ffn_mms(e, p=par):
                inst = None
                for dc in range(8):
                    inst = e.matmul(PB4[:, :], QLF[p][:, 128 * dc:128 * (dc + 1)],
                                    QRHS[:, 512 * dc:512 * (dc + 1)],
                                    start=(dc == 0), stop=(dc == 7))
                return inst
            tFFN = sch.add(Tn, ffn_mms, deps=[tQLF, tQR] + pw["PB4"])
            pw["PB4"] = []
            reads(QLF[par], tFFN)
            tE2 = sch.add(V, lambda e, p=par, s=p3: e.scalar_tensor_tensor(
                E2[p][:, :], PB4[:, :], STAT[s][:, 9:10], RSB[:, :],
                ALU.mult, ALU.mult), deps=[tFFN, tTY1, tRSB] + war(E2, par))
            pw["PB4"].append(tE2)

            def attn_grp(e, half, bank, par_sel, p=par):
                inst = None
                for hh_ in range(par_sel, 4, 2):
                    h = 4 * half + hh_
                    bp = 64 * (h % 2)
                    inst = e.matmul(
                        bank[:, 128 * hh_:128 * (hh_ + 1)],
                        KFM[p][bp:bp + 64,
                               128 * (h // 2):128 * (h // 2) + 128],
                        QFM[p][bp:bp + 64,
                               128 * (h // 2):128 * (h // 2) + 128],
                        start=True, stop=True)
                return inst
            tATT0e = sch.add(Tn, lambda e: attn_grp(e, 0, PB5, 0),
                             deps=[tKFM, tQFM] + pw["PB5"], pe_fence=True)
            tATT0 = sch.add(Tn, lambda e: attn_grp(e, 0, PB5, 1),
                            deps=[tATT0e], pe_fence=True)
            pw["PB5"] = []
            tATT1e = sch.add(Tn, lambda e: attn_grp(e, 1, PB2, 0),
                             deps=[tKFM, tQFM, tZA] + pw["PB2"], pe_fence=True)
            tATT1 = sch.add(Tn, lambda e: attn_grp(e, 1, PB2, 1),
                            deps=[tATT1e], pe_fence=True)
            pw["PB2"] = []
            reads(KFM[par], tATT0, tATT1)

            def mem_grp(e, p=par, par_sel=0):
                inst = None
                for h in range(par_sel, 8, 2):
                    bp = 64 * (h % 2)
                    inst = e.matmul(PB0[:, 64 * h:64 * (h + 1)],
                                    QFM[p][bp:bp + 64,
                                           128 * (h // 2):128 * (h // 2) + 128],
                                    SBF[bp:bp + 64, 64 * (h // 2):64 * (h // 2) + 64],
                                    start=True, stop=True)
                return inst
            tMEMe = sch.add(Tn, lambda e, f=mem_grp: f(e, par_sel=0),
                            deps=[tQFM, tSBF_prev] + pw["PB0"], pe_fence=True)
            tMEM = sch.add(Tn, lambda e, f=mem_grp: f(e, par_sel=1),
                           deps=[tMEMe], pe_fence=True)
            pw["PB0"] = []
            reads(QFM[par], tATT0, tATT1, tMEM)

            def a_mms(e, p=par):
                e.matmul(PB7[0:64, 272:276], ONES1[0:1, 0:64],
                         AROW[p][0:1, 0:8:2], start=True, stop=True)
                e.nop()
                return e.matmul(PB7[64:128, 272:276], ONES1[0:1, 0:64],
                                AROW[p][0:1, 1:8:2], start=True, stop=True,
                                tile_position=(0, 64))
            tA = sch.add(Tn, a_mms, deps=[tAR])
            reads(AROW[par], tA)
            tAE = sch.add(V, lambda e, p=par: e.tensor_copy(
                ASB[p][:, :], PB7[:, 272:276]), deps=[tA] + war(ASB, par))
            pw["PB7"].append(tAE)
            tTB = sch.add(Tn, lambda e, p=par: e.transpose(
                PTRB[0:24, 0:128], PACKB[p][:, :], IDENT[:, :]),
                deps=[tPB_] + pw["PTRB"], pe_fence=True)
            reads(PACKB[par], tTB)
            tRVT = sch.add(V, lambda e, p=par: e.tensor_copy(
                RVT[p][:, :], PTRB[0:24, 0:128]), deps=[tTB] + war(RVT, par))
            pw["PTRB"].append(tRVT)


            tECV = sch.add(Sc, lambda e, p=par: e.activation(
                ECV[p][:, :], CV[p][:, :], AF.Exp),
                deps=[tTY3] + war(ECV, par))
            reads(CV[par], tECV)
            tVB1 = sch.add(V, lambda e, p=par: e.tensor_mul(
                VBT[p][:, :], VS[p][:, :], BETA[p][:, :]),
                deps=[tVS, tBE] + war(VBT, par))
            tVB = sch.add(V, lambda e, p=par: e.tensor_tensor(
                VB[p][:, :].rearrange("p (h d) -> p h d", h=8),
                VBT[p][:, :].rearrange("p (h d) -> p h d", h=8),
                ECV[p][:, :].unsqueeze(2).broadcast_to([128, 8, 64]),
                ALU.mult), deps=[tVB1, tECV] + war(VB, par))
            reads(VBT[par], tVB)
            reads(ECV[par], tVB)
            tVB2 = sch.add(V, lambda e, p=par: e.tensor_tensor(
                VB2[p][:, :].rearrange("p (h d) -> p h d", h=8),
                VS[p][:, :].rearrange("p (h d) -> p h d", h=8),
                BMR[p][:, :].unsqueeze(2).broadcast_to([128, 8, 64]),
                ALU.mult), deps=[tVS, tTY3] + war(VB2, par))
            reads(VS[par], tVB, tVB2)
            reads(BETA[par], tVB, tBM8)
            reads(BMR[par], tVB2)

            def t0_mms(e, half, bank, p=par):
                inst = None
                for hh_ in range(4):
                    h = 4 * half + hh_
                    e.matmul(bank[:, 128 * hh_:128 * (hh_ + 1)],
                             SEL24[:, 128 * h:128 * (h + 1)], RVT[p][:, :],
                             start=True, stop=False)
                    inst = e.matmul(bank[:, 128 * hh_:128 * (hh_ + 1)],
                                    IDENT[:, :], MASKN4[:, 0:128],
                                    start=False, stop=True)
                return inst
            tT00 = sch.add(Tn, lambda e: t0_mms(e, 0, PB7),
                           deps=[tRVT, tMSB, tAR, tAE] + pw["PB7"])
            pw["PB7"] = []
            tT01 = sch.add(Tn, lambda e: t0_mms(e, 1, PB3),
                           deps=[tRVT] + pw["PB3"])
            pw["PB3"] = []
            reads(RVT[par], tT00, tT01)

            def e_acts(e, half, bank, p=par):
                return e.activation(
                    ESB[p][:, 512 * half:512 * (half + 1)],
                    bank[:, 0:512], AF.Exp)
            tEACT0 = sch.add(Sc, lambda e: e_acts(e, 0, PB7),
                             deps=[tT00, tTY3] + war(ESB, par))
            pw["PB7"].append(tEACT0)
            tEACT1 = sch.add(Sc, lambda e: e_acts(e, 1, PB3),
                             deps=[tT01, tTY3])
            pw["PB3"].append(tEACT1)
            tAPL0 = sch.add(V, lambda e, p=par: e.tensor_mul(
                ATS[p][:, 0:512], PB5[:, :], ESB[p][:, 0:512]),
                deps=[tATT0, tEACT0] + war(ATS, par))
            pw["PB5"].append(tAPL0)
            tAPL1 = sch.add(V, lambda e, p=par: e.tensor_mul(
                ATS[p][:, 512:1024], PB2[:, :], ESB[p][:, 512:1024]),
                deps=[tATT1, tEACT1])
            pw["PB2"].append(tAPL1)
            reads(ESB[par], tAPL0, tAPL1)

            def del_grp(e, p=par, par_sel=0):
                inst = None
                for h in range(par_sel, 8, 2):
                    bp = 64 * (h % 2)
                    inst = e.matmul(PB1[bp:bp + 64,
                                        64 * (h // 2):64 * (h // 2) + 64],
                                    KTM[p][:, 64 * h:64 * (h + 1)],
                                    VB2[p][:, 64 * h:64 * (h + 1)],
                                    start=True, stop=True,
                                    tile_position=(0, bp))
                return inst
            tDELe = sch.add(Tn, lambda e, f=del_grp: f(e, par_sel=0),
                            deps=[tKTM, tVB2] + pw["PB1"], pe_fence=True)
            tDEL = sch.add(Tn, lambda e, f=del_grp: f(e, par_sel=1),
                           deps=[tDELe], pe_fence=True)
            pw["PB1"] = []
            reads(KTM[par], tDEL)
            reads(VB2[par], tDEL)

            def yin_mms(e, half, p=par):
                inst = None
                for h in range(4 * half, 4 * half + 4):
                    inst = e.matmul(PB5[:, 64 * h:64 * (h + 1)],
                                    ATS[p][:, 128 * h:128 * (h + 1)],
                                    VB[p][:, 64 * h:64 * (h + 1)],
                                    start=True, stop=True)
                return inst
            tYIN0 = sch.add(Tn, lambda e: yin_mms(e, 0),
                            deps=[tAPL0, tVB] + pw["PB5"])
            pw["PB5"] = []
            tYIN1 = sch.add(Tn, lambda e: yin_mms(e, 1), deps=[tAPL1])
            reads(ATS[par], tYIN0, tYIN1)
            reads(VB[par], tYIN0, tYIN1)

            tSS = sch.add(V, lambda e, p=par: e.tensor_tensor(
                S[:, :].rearrange("p (h d) -> p h d", h=4),
                S[:, :].rearrange("p (h d) -> p h d", h=4),
                ASB[p][:, :].unsqueeze(2).broadcast_to([128, 4, 64]),
                ALU.mult), deps=[tAE, tS_prev])
            reads(ASB[par], tSS)
            tSA = sch.add(V, lambda e: e.tensor_add(
                S[:, :], S[:, :], PB1[:, 0:256]), deps=[tSS, tDEL])
            pw["PB1"].append(tSA)
            tS_prev = tSA
            tSBF = sch.add(V, lambda e: e.tensor_copy(SBF[:, :], S[:, :]),
                           deps=[tSA, tMEM])
            tSBF_prev = tSBF

            tO1 = sch.add(V, lambda e, p=par: e.tensor_tensor(
                O1[p][:, :].rearrange("p (h d) -> p h d", h=8),
                PB0[:, :].rearrange("p (h d) -> p h d", h=8),
                CR[p][:, :].unsqueeze(2).broadcast_to([128, 8, 64]),
                ALU.mult), deps=[tMEM, tTY3] + war(O1, par))
            pw["PB0"].append(tO1)
            reads(CR[par], tO1)
            tY2 = sch.add(V, lambda e, p=par: e.tensor_add(
                Y2[p][:, :], O1[p][:, :], PB5[:, :]),
                deps=[tO1, tYIN0, tYIN1] + war(Y2, par))
            pw["PB5"].append(tY2)
            reads(O1[par], tY2)
            reads(STAT[p3], tVS, tEB, tZA, tTY3, tE2)
            tT3 = sch.add(Gp, lambda e, p=par: e.tensor_add(
                T3[p][:, :], E2[p][:, :], Y2[p][:, :]),
                deps=[tE2, tY2] + war(T3, par))
            reads(E2[par], tT3)
            reads(Y2[par], tT3)
            tOB = sch.add(Gp, lambda e, p=par, x=p4: e.tensor_add(
                OUTB[p][:, :], T3[p][:, :], XT[x][:, 0:FS]),
                deps=[tT3, tXT] + war(OUTB, par))
            reads(T3[par], tOB)
            reads(XT[p4], tOB)
            tOD = sch.add(Sy, lambda e, c=cidx, p=par: e.dma_start(
                out_ch[c], OUTB[p][:, :]), deps=[tOB], dma=True,
                sem=f"dout{par}")
            reads(OUTB[par], tOD)
            tOUT_dma2 = tOUT_dma
            tOUT_dma = tOD

        for c in range(min(4, nch)):
            dma(c)
        SD = {}
        for c in range(min(2, nch)):
            SD[c] = stats(c)
            fixtrail(c, SD[c])
        for c in range(nch):
            if c + 2 < nch:
                SD[c + 2] = stats(c + 2)
            main(c, SD.pop(c))
            if c + 2 < nch:
                fixtrail(c + 2, SD[c + 2])
            if c + 4 < nch:
                dma(c + 4)

        sch.add(Sy, lambda e: e.nop(),
                deps=[tOUT_dma, tOUT_dma2, tQR, tROWD, tFIXDONE])

        sch.resolve()

        @block.sync
        def _(eng):
            sch.emit_one(Sy, eng, sems)

        @block.gpsimd
        def _(eng):
            sch.emit_one(Gp, eng, sems)

        @block.vector
        def _(eng):
            sch.emit_one(V, eng, sems)

        @block.scalar
        def _(eng):
            sch.emit_one(Sc, eng, sems)

        @block.tensor
        def _(eng):
            sch.emit_one(Tn, eng, sems)

    return nc


_NC_CACHE = None


def kernel(L0_attn_x, L0_attn_Wq, L0_attn_bq, L0_attn_Wk, L0_attn_bk,
           L0_attn_Wv, L0_attn_bv, L0_attn_Wbeta, L0_attn_bbeta,
           L0_attn_Walpha, L0_attn_balpha, L0_ffn_rhs):
    global _NC_CACHE
    x = np.ascontiguousarray(np.asarray(L0_attn_x, dtype=np.float32))
    Ws = {"q": np.asarray(L0_attn_Wq, np.float32),
          "k": np.asarray(L0_attn_Wk, np.float32),
          "v": np.asarray(L0_attn_Wv, np.float32),
          "be": np.asarray(L0_attn_Wbeta, np.float32),
          "al": np.asarray(L0_attn_Walpha, np.float32)}
    Bs = {"q": np.asarray(L0_attn_bq, np.float32),
          "k": np.asarray(L0_attn_bk, np.float32),
          "v": np.asarray(L0_attn_bv, np.float32),
          "be": np.asarray(L0_attn_bbeta, np.float32),
          "al": np.asarray(L0_attn_balpha, np.float32)}
    rhs = np.asarray(L0_ffn_rhs, np.float32)

    if _NC_CACHE is None:
        _NC_CACHE = build_program()
    nc = _NC_CACHE

    in_maps = []
    for core in range(8):
        b = core // 2
        hh = core % 2
        F0 = FS * hh
        perm = np.r_[F0:F0 + FS, 0:F0, F0 + FS:D]
        xp = np.ascontiguousarray(x[b][:, perm])
        m = {
            "xt": xp,
            "xf": np.ascontiguousarray(xp.T),
            "rhs": np.ascontiguousarray(rhs[perm][:, F0:F0 + FS]),
        }
        for p in PROJ:
            m[f"w{p}"] = np.ascontiguousarray(Ws[p][F0:F0 + FS, :][:, perm].T)
            m[f"b{p}"] = np.ascontiguousarray(Bs[p][F0:F0 + FS]).reshape(1, FS)
        in_maps.append(m)

    res = run_bass_kernel_spmd(nc, in_maps, core_ids=list(range(8)))
    out = np.empty((B, T, D), np.float32)
    for core in range(8):
        b = core // 2
        hh = core % 2
        F0 = FS * hh
        out[b, :, F0:F0 + FS] = np.asarray(res.results[core]["out"])
    return out
